# version 04 (final)
# baseline (speedup 1.0000x reference)
import os
import numpy as np
import contextlib
import concourse.bass as bass
import concourse.mybir as mybir
from concourse.bass_utils import run_bass_kernel_spmd

F32 = mybir.dt.float32
BF16 = mybir.dt.bfloat16
I32 = mybir.dt.int32
AF = mybir.ActivationFunctionType
ALU = mybir.AluOpType
AX = mybir.AxisListType.X

D = 4096
WQ = 1024
H = 16
DH = 64
IN = 11008
NI = 86
TC = 2048
ACH = 1024
NA = TC // ACH
FCH = 512
NF = TC // FCH
EPS = 1e-5
QSC = 0.125
STAGE = os.environ.get("ATHENA_STAGE", "full")
F3X = int(os.environ.get("ATHENA_F3X", "3"))
FLIM = int(os.environ.get("ATHENA_FLIM", "99"))


class Sem:
    def __init__(self, es, nc, name, dma=False):
        self.h = es.enter_context(nc.semaphore(name))
        self.name = name
        self.n = 0
        self.step = 16 if dma else 1


class Prog:
    ENGS = ("tensor", "vector", "scalar", "gpsimd", "sync")

    def __init__(self, nc, blk):
        self.nc = nc
        self.blk = blk
        self.q = {e: [] for e in self.ENGS}
        self.lastwait = {}

    def flush(self):
        for e in self.ENGS:
            fns = self.q[e]
            if fns:
                def run(eng, fns=fns):
                    for f in fns:
                        f(eng)
                getattr(self.blk, e)(run)
        self.q = {e: [] for e in self.ENGS}

    def w(self, eng, sem, val):
        if not val or val <= 0:
            return
        k = (eng, sem.name)
        if self.lastwait.get(k, 0) >= val:
            return
        self.lastwait[k] = val
        self.q[eng].append(lambda e, s=sem.h, v=val: e.wait_ge(s, v))

    def do(self, eng, fn, sem=None, waits=()):
        for s, v in waits:
            self.w(eng, s, v)
        if sem is None:
            self.q[eng].append(fn)
            return None
        sem.n += sem.step
        val = sem.n
        self.q[eng].append(lambda e, f=fn, s=sem.h, k=sem.step: f(e).then_inc(s, k))
        return val

    def barrier(self, sems):
        for e in self.ENGS:
            for s in sems:
                self.w(e, s, s.n)


class Alloc:

    def __init__(self, nc):
        self.nc = nc
        self.n = 0

    def get(self, shape, dt, side="left"):
        self.n += 1
        cm = self.nc.sbuf_tensor("ab%d" % self.n, list(shape), dt, side=side)
        h = cm.__enter__()
        return h, cm

    def free(self, *cms):
        for cm in cms:
            cm.__exit__(None, None, None)


def build_nc():
    nc = bass.Bass("TRN2", target_bir_lowering=False)
    es = contextlib.ExitStack()

    def din(name, shape, dt=F32):
        return nc.dram_tensor(name, list(shape), dt, kind="ExternalInput").ap()

    xh = din("xh", [TC + 2, D])
    conv_w = din("conv_w", [3, D, 3 * WQ])
    wq_d = din("wq", [WQ, H * DH])
    wk_d = din("wk", [WQ, H * DH])
    wv_d = din("wv", [WQ, H * DH])
    wo_d = din("wo", [H * DH, WQ])
    pw_d = din("pw", [WQ, D])
    conv_b = din("conv_b", [1, 3 * WQ])
    bq_d = din("bq", [1, H * DH]); bk_d = din("bk", [1, H * DH]); bv_d = din("bv", [1, H * DH])
    bo_d = din("bo", [1, WQ]); pb_d = din("pb", [1, D])
    gq_d = din("g_qw", [512, IN], I32); gs_d = din("g_sc", [32, IN]); gz_d = din("g_qz", [32, 1376], I32)
    uq_d = din("u_qw", [512, IN], I32); us_d = din("u_sc", [32, IN]); uz_d = din("u_qz", [32, 1376], I32)
    dq_d = din("d_qw", [1376, D], I32); ds_d = din("d_sc", [86, D]); dz_d = din("d_qz", [86, 512], I32)
    eye_d = din("eye", [128, 128])
    mask_d = din("mask", [128, 128])
    ones_d = din("ones", [1, 512])
    c32_d = din("c32", [128, 32 * 32])
    c86_d = din("c86", [128, 86 * 86])
    band_d = din("band", [8, 128])

    y_d = nc.dram_tensor("y", [TC, D], F32, kind="ExternalOutput").ap()

    wg_s = nc.dram_tensor("wg_s", [NI, 128, 32 * 128], BF16, kind="Internal").ap()
    wu_s = nc.dram_tensor("wu_s", [NI, 128, 32 * 128], BF16, kind="Internal").ap()
    wd_s = nc.dram_tensor("wd_s", [32, 128, NI * 128], BF16, kind="Internal").ap()
    szg_s = nc.dram_tensor("szg_s", [32, IN], BF16, kind="Internal").ap()
    szu_s = nc.dram_tensor("szu_s", [32, IN], BF16, kind="Internal").ap()
    szd_s = nc.dram_tensor("szd_s", [86, D], BF16, kind="Internal").ap()
    x2_s = nc.dram_tensor("x2_s", [TC, D], F32, kind="Internal").ap()

    blk = es.enter_context(nc.Block())
    P = Prog(nc, blk)
    A = Alloc(nc)
    _u = [0]

    def sbt(name, shape, dt):
        _u[0] += 1
        return nc.sbuf_tensor("%s_%d" % (name, _u[0]), list(shape), dt)

    def pst(name, shape, dt):
        _u[0] += 1
        return nc.psum_tensor("%s_%d" % (name, _u[0]), list(shape), dt)

    pe = Sem(es, nc, "pe"); ve = Sem(es, nc, "ve"); se = Sem(es, nc, "se")
    gq = Sem(es, nc, "gq", dma=True)
    ALLS = [pe, ve, se, gq]
    QS = {}

    def qsem(name):
        if name not in QS:
            QS[name] = Sem(es, nc, name, dma=True)
            ALLS.append(QS[name])
        return QS[name]

    gstk, _c1 = A.get([128, 1280], F32)
    eye32 = gstk[:, 0:128]
    mask32 = gstk[:, 128:256]
    eyebf, _c2 = A.get([128, 128], BF16)
    onesbf, _c3 = A.get([1, 512], BF16)
    c32bf, _c4 = A.get([128, 1024], BF16)
    haloL, _c5 = A.get([128, 32], BF16)
    epsc, _c6 = A.get([128, 1], F32)
    NBIAS = 3 * WQ + 3 * H * DH + WQ + D
    OFF_CB, OFF_BQ, OFF_BK, OFF_BV, OFF_BO, OFF_PB = 0, 3072, 4096, 5120, 6144, 7168
    biasbf, cm_biasbf = A.get([1, NBIAS], BF16)
    bias32, cm_bias32 = A.get([1, NBIAS], F32, side="right")
    ones32, cm_ones32 = A.get([1, 512], F32, side="right")

    P.do("vector", lambda e: e.memset(epsc[:], EPS), ve)
    cq = qsem("cq")
    P.do("sync", lambda e: e.dma_start(out=gstk[:, 0:128], in_=eye_d[:]), cq)
    P.do("sync", lambda e: e.dma_start(out=gstk[:, 128:256], in_=mask_d[:]), cq)
    P.do("sync", lambda e: e.dma_start(out=ones32[:], in_=ones_d[:]), cq)
    P.do("sync", lambda e: e.dma_start(out=gstk[:, 256:1280], in_=c32_d[:]), cq)
    for srcd, off, ln in [(conv_b, OFF_CB, 3072), (bq_d, OFF_BQ, 1024), (bk_d, OFF_BK, 1024),
                          (bv_d, OFF_BV, 1024), (bo_d, OFF_BO, 1024), (pb_d, OFF_PB, 4096)]:
        P.do("sync", lambda e, s=srcd, o=off, n=ln: e.dma_start(out=bias32[0:1, o:o + n], in_=s[:]), cq)
    t_cq = cq.n
    t_eye = t_mask = t_cq
    t_eyebf = P.do("vector", lambda e: e.tensor_copy(out=eyebf[:], in_=gstk[:, 0:128]), ve, waits=[(cq, t_cq)])
    t_onesbf = P.do("vector", lambda e: e.tensor_copy(out=onesbf[:], in_=ones32[:]), ve)
    t_c32bf = P.do("vector", lambda e: e.tensor_copy(out=c32bf[:], in_=gstk[:, 256:1280]), ve)
    t_biasbf = P.do("vector", lambda e: e.tensor_copy(out=biasbf[:], in_=bias32[:]), ve)
    P.flush()
    A.free(cm_ones32, cm_bias32)

    def dequant(qw_dram, sc_dram, qz_dram, out_dram, sz_out, nG, K8, N):
        P.barrier(ALLS)
        rowlen = nG * 128
        nb_full = K8 // 128
        rem = K8 - nb_full * 128
        ncol = (N + 511) // 512
        with contextlib.ExitStack() as ps:
            qwt = [ps.enter_context(sbt("qwt%d" % i, [128, 512], I32)) for i in range(2)]
            w4t = ps.enter_context(sbt("w4t", [128, 512], I32))
            wdq = [ps.enter_context(sbt("wdq%d" % i, [128, 8, 512], BF16)) for i in range(2)]
            NCC = N // 4
            scb = ps.enter_context(sbt("scb", [8, N], F32))
            scc = ps.enter_context(sbt("scc", [nG, NCC], F32))
            qzt = ps.enter_context(sbt("qzt", [nG, NCC // 8], I32))
            zti = ps.enter_context(sbt("zti", [nG, NCC], I32))
            zt = ps.enter_context(sbt("zt", [nG, NCC], F32))
            szb = ps.enter_context(sbt("szb", [nG, NCC], BF16))
            bandt = ps.enter_context(sbt("bandt", [8, 128], F32))
            pss = [ps.enter_context(pst("pss%d" % i, [128, 512], F32)) for i in range(2)]

            dqm = qsem("dqm")
            for _s in range(2):
                qsem("st%d" % _s); qsem("qt%d" % _s)
            P.do("scalar", lambda e: e.dma_start(out=bandt[:], in_=band_d[:]), qsem("bnd"))
            t_band = QS["bnd"].n
            szw_free = 0
            szw_dma = 0
            for cc in range(4):
                P.do("scalar", lambda e, cc=cc: e.dma_start(
                    out=scc[:], in_=sc_dram[:, NCC * cc:NCC * cc + NCC]), dqm,
                    waits=[(ve, szw_free)])
                P.do("scalar", lambda e, cc=cc: e.dma_start(
                    out=qzt[:], in_=qz_dram[:, (NCC // 8) * cc:(NCC // 8) * cc + NCC // 8]), dqm)
                t_ld = dqm.n
                tz = szw_free
                for j2 in range(8):
                    tz = P.do("vector", lambda e, j=j2: e.tensor_scalar(
                        out=bass.AP(zti, j, [[NCC, nG], [8, NCC // 8]]),
                        in0=qzt[:], scalar1=4 * j, scalar2=15,
                        op0=ALU.logical_shift_right, op1=ALU.bitwise_and), ve,
                        waits=[(QS["dqm"], t_ld), (ve, tz)])
                t1 = P.do("vector", lambda e: e.tensor_scalar(out=zt[:], in0=zti[:], scalar1=-1.0, scalar2=-1.0,
                                                              op0=ALU.mult, op1=ALU.add), ve, waits=[(ve, tz)])
                t2 = P.do("vector", lambda e: e.tensor_tensor(out=szb[:], in0=zt[:], in1=scc[:], op=ALU.mult),
                          ve, waits=[(ve, t1), (qsem("szw"), szw_dma)])
                szw_dma = P.do("gpsimd", lambda e, cc=cc: e.dma_start(
                    out=sz_out[:, NCC * cc:NCC * cc + NCC], in_=szb[:]), qsem("szw"), waits=[(ve, t2)])
                szw_free = t2

            qw_free = [0, 0]
            st_free = [0, 0]
            pss_free = [0, 0]
            scb_free = 0
            it = 0
            for b in range(nb_full + (1 if rem else 0)):
                rows = 128 if b < nb_full else rem
                grows = (rows + 15) // 16
                t_scb = P.do("scalar", lambda e, b=b, g=grows: e.dma_start(
                    out=scb[0:g, :], in_=sc_dram[8 * b:8 * b + g, :]), qsem("scb"), waits=[(pe, scb_free)])
                for nb2 in range(ncol):
                    w2 = min(512, N - 512 * nb2)
                    nii = w2 // 128
                    sl = it % 2
                    pp = it % 2
                    it += 1
                    t_qw = P.do("sync", lambda e, b=b, nb=nb2, r=rows, s=sl, w2=w2: e.dma_start(
                        out=qwt[s][0:r, 0:w2],
                        in_=qw_dram[128 * b:128 * b + r, 512 * nb:512 * nb + w2]),
                        qsem("qt%d" % sl), waits=[(ve, qw_free[sl])])
                    t_sex = P.do("tensor", lambda e, r=rows, g=grows, nb=nb2, pp=pp, w2=w2: e.matmul(
                        pss[pp][0:r, 0:w2], bandt[0:g, 0:r], scb[0:g, 512 * nb:512 * nb + w2],
                        start=True, stop=True), pe,
                        waits=[(QS["scb"], t_scb), (QS["bnd"], t_band), (ve, pss_free[pp])])
                    scb_free = t_sex
                    tm = 0
                    for j in range(8):
                        tw = P.do("vector", lambda e, r=rows, s=sl, j=j, w2=w2: e.tensor_scalar(
                            out=w4t[0:r, 0:w2], in0=qwt[s][0:r, 0:w2], scalar1=4 * j, scalar2=15,
                            op0=ALU.logical_shift_right, op1=ALU.bitwise_and),
                            ve, waits=[(QS["qt%d" % sl], t_qw), (ve, tm),
                                       (QS["st%d" % sl], st_free[sl])])
                        tm = P.do("vector", lambda e, r=rows, s=sl, j=j, pp=pp, w2=w2: e.tensor_tensor(
                            out=bass.AP(wdq[s], j * w2, [[8 * 512, r], [1, w2]]),
                            in0=w4t[0:r, 0:w2], in1=pss[pp][0:r, 0:w2], op=ALU.mult),
                            ve, waits=[(ve, tw), (pe, t_sex)])
                    qw_free[sl] = tm
                    pss_free[pp] = tm
                    tsc2 = 0
                    for u in range(grows):
                        oap = bass.AP(out_dram.tensor,
                                      nb2 * 512 * rowlen + 1024 * b + u * 128,
                                      [[8 * rowlen, 16], [rowlen, 8], [128 * rowlen, nii], [1, 128]])
                        iap = bass.AP(wdq[sl], u * 16 * 4096, [[4096, 16], [w2, 8], [128, nii], [1, 128]])
                        tsc2 = P.do("sync" if u % 2 == 0 else "scalar",
                                    lambda e, a=oap, ia=iap: e.dma_start(out=a, in_=ia),
                                    qsem("st%d" % sl), waits=[(ve, tm)])
                    st_free[sl] = tsc2
            P.barrier(ALLS)
            P.flush()

    if STAGE in ("full", "deq"):
        dequant(gq_d, gs_d, gz_d, wg_s, szg_s, 32, 512, IN)
        dequant(uq_d, us_d, uz_d, wu_s, szu_s, 32, 512, IN)
        dequant(dq_d, ds_d, dz_d, wd_s, szd_s, 86, 1376, D)

    def ln_pass(src_fn, ntiles, emit_tile, xbuf, sqb, stat, xlntm, ld_eng, ld_sems):
        prev = {"se": 0, "ve": 0, "free0": 0, "free1": 0}
        for i in range(ntiles):
            half = i % 2
            ld_sem = ld_sems[half]
            tl = P.do(ld_eng, lambda e, i=i, h=half: e.dma_start(
                out=xbuf[:, h, :], in_=src_fn(i)), ld_sem,
                waits=[(se, prev["free%d" % half])])
            t1 = P.do("vector", lambda e, h=half: e.reduce_sum(out=stat[:, 0:1], in_=xbuf[:, h, :], axis=AX),
                      ve, waits=[(ld_sem, tl), (ve, prev["ve"])])
            t2 = P.do("vector", lambda e: e.tensor_scalar(out=stat[:, 1:2], in0=stat[:, 0:1],
                                                          scalar1=-1.0 / D, scalar2=None, op0=ALU.mult),
                      ve, waits=[(ve, t1)])
            t3 = P.do("scalar", lambda e, h=half: e.activation(out=sqb[:], in_=xbuf[:, h, :], func=AF.Square,
                                                               bias=stat[:, 1:2], scale=1.0,
                                                               accum_out=stat[:, 2:3]),
                      se, waits=[(ve, t2), (se, prev["se"])])
            t4 = P.do("scalar", lambda e: e.activation(out=stat[:, 3:4], in_=stat[:, 2:3], func=AF.Sqrt,
                                                       bias=epsc[:], scale=1.0 / D), se, waits=[(se, t3)])
            t5 = P.do("vector", lambda e: e.reciprocal(out=stat[:, 4:5], in_=stat[:, 3:4]), ve, waits=[(se, t4)])
            t6 = P.do("vector", lambda e: e.tensor_tensor(out=stat[:, 5:6], in0=stat[:, 1:2], in1=stat[:, 4:5],
                                                          op=ALU.mult), ve, waits=[(ve, t5)])
            t7 = P.do("scalar", lambda e, h=half: e.activation(out=xlntm[:, h, :], in_=xbuf[:, h, :], func=AF.Identity,
                                                               scale=stat[:, 4:5], bias=stat[:, 5:6]),
                      se, waits=[(ve, t6)])
            prev["se"] = t7
            prev["ve"] = t6
            emit_tile(i, t7, half)
            prev["free%d" % half] = t7
        return prev

    def ln_transposes(i, t7, half, xlntm, dst_tensor, dst_cols, col0, tps, evtoks):
        for g4 in range(8):
            tpp = tps[g4 % 2]
            tt = None
            for k4 in range(4):
                a = g4 * 4 + k4
                tt = P.do("tensor", lambda e, h=half, a=a, k4=k4, tpp=tpp: e.transpose(
                    tpp[:, 128 * k4:128 * k4 + 128], xlntm[:, h, 128 * a:128 * a + 128], eyebf[:]),
                    pe if k4 == 3 else None,
                    waits=[(se, t7), (ve, evtoks.get(g4 % 2, 0)), (ve, t_eyebf)])
            dst = dst_tensor[:, g4 * 4:g4 * 4 + 4, col0 + 128 * i:col0 + 128 * i + 128]
            tev = P.do("vector", lambda e, tpp=tpp, d=dst: e.tensor_copy(
                out=d, in_=tpp[:].rearrange("p (a n) -> p a n", n=128)), ve, waits=[(pe, tt)])
            evtoks[g4 % 2] = tev

    x2_tok = 0
    for ch in range(NA):
        base_row = 1 + ACH * ch
        CW = ACH + 2
        xln_fm, cm_xln = A.get([128, 32, CW], BF16)
        qkv_fm, cm_qkv = A.get([128, 24, ACH], BF16, side="right")

        xbuf, cm_xbuf = A.get([128, 2, D], F32)
        sqb, cm_sqb = A.get([128, D], F32)
        stat, cm_stat = A.get([128, 16], F32)
        xlntm, cm_xlntm = A.get([128, 2, D], BF16)
        with contextlib.ExitStack() as s1:
            tps = [s1.enter_context(pst("tps%d" % i, [128, 512], BF16)) for i in range(2)]
            hps = s1.enter_context(pst("hps", [128, 64], BF16))
            P.barrier(ALLS)
            if ch > 0:
                P.do("vector", lambda e: e.tensor_copy(out=xln_fm[:, :, 0:1],
                                                       in_=haloL[:].rearrange("p (a o) -> p a o", o=1)), ve)
            evtoks = {}
            ln_pass(lambda i: xh[base_row + 128 * i: base_row + 128 * (i + 1), :], ACH // 128,
                    lambda i, t7, half: ln_transposes(i, t7, half, xlntm, xln_fm, CW, 1, tps, evtoks),
                    xbuf, sqb, stat, xlntm, "gpsimd", [qsem("xq0"), qsem("xq1")])
            halos = []
            if ch == 0:
                halos.append((0, 0))
            halos.append((ACH + 1, base_row + ACH))
            hrow = xbuf[0:1, 0, :]
            hsq = sqb[0:1, :]
            hst = stat[0:1, 8:16]
            hln = xlntm[0:1, 0, :]
            hev = 0
            for colx, rowx in halos:
                tl = P.do("gpsimd", lambda e, r=rowx: e.dma_start(out=hrow, in_=xh[r:r + 1, :]), qsem("hq"),
                          waits=[(se, se.n), (ve, ve.n)])
                t1 = P.do("vector", lambda e: e.reduce_sum(out=hst[:, 0:1], in_=hrow, axis=AX), ve,
                          waits=[(QS["hq"], tl)])
                t2 = P.do("vector", lambda e: e.tensor_scalar(out=hst[:, 1:2], in0=hst[:, 0:1],
                                                              scalar1=-1.0 / D, scalar2=None, op0=ALU.mult),
                          ve, waits=[(ve, t1)])
                t3 = P.do("scalar", lambda e: e.activation(out=hsq, in_=hrow, func=AF.Square,
                                                           bias=hst[:, 1:2], scale=1.0, accum_out=hst[:, 2:3]),
                          se, waits=[(ve, t2), (se, se.n)])
                t4 = P.do("scalar", lambda e: e.activation(out=hst[:, 3:4], in_=hst[:, 2:3], func=AF.Sqrt,
                                                           bias=epsc[0:1, :], scale=1.0 / D), se, waits=[(se, t3)])
                t5 = P.do("vector", lambda e: e.reciprocal(out=hst[:, 4:5], in_=hst[:, 3:4]), ve, waits=[(se, t4)])
                t6 = P.do("vector", lambda e: e.tensor_tensor(out=hst[:, 5:6], in0=hst[:, 1:2],
                                                              in1=hst[:, 4:5], op=ALU.mult), ve, waits=[(ve, t5)])
                t7 = P.do("scalar", lambda e: e.activation(out=hln, in_=hrow, func=AF.Identity,
                                                           scale=hst[:, 4:5], bias=hst[:, 5:6]), se,
                          waits=[(ve, t6)])
                tt = None
                for a in range(32):
                    tt = P.do("tensor", lambda e, a=a: e.transpose(
                        hps[:, 2 * a:2 * a + 1], hln[0:1, 128 * a:128 * a + 128], eyebf[0:1, 0:1]),
                        pe if a == 31 else None, waits=[(se, t7), (ve, hev)])
                hev = P.do("vector", lambda e, c=colx: e.tensor_copy(
                    out=xln_fm[:, :, c:c + 1],
                    in_=hps[:].rearrange("p (a b) -> p a b", b=2)[:, :, 0:1]),
                    ve, waits=[(pe, tt)])
            P.barrier(ALLS)
            P.flush()
        A.free(cm_xlntm, cm_stat, cm_sqb, cm_xbuf)

        with contextlib.ExitStack() as s2:
            NSL = 3
            cwraw = s2.enter_context(sbt("cwraw", [128, NSL, 2048], F32))
            cwbf = s2.enter_context(sbt("cwbf", [128, NSL, 2048], BF16))
            cps = [s2.enter_context(pst("cps%d" % i, [128, 512], F32)) for i in range(8)]
            slot_ld = [0] * NSL
            slot_mm = [0] * NSL
            set_free = [0, 0]
            sli = 0
            for grp in range(12):
                pset = grp % 2
                pss4 = cps[4 * pset:4 * pset + 4]
                first = True
                for w in range(3):
                    for kh in range(4):
                        sl = sli % NSL
                        sli += 1
                        tld = P.do("sync", lambda e, w=w, kh=kh, g=grp, sl=sl: e.dma_start(
                            out=cwraw[:, sl, :].rearrange("p (a n) -> p a n", n=256),
                            in_=conv_w[w, 1024 * kh:1024 * kh + 1024, 256 * g:256 * g + 256]
                            .rearrange("(a p) n -> p a n", p=128)),
                            qsem("cv%d" % sl), waits=[(se, slot_ld[sl])])
                        tcast = P.do("scalar", lambda e, sl=sl: e.activation(
                            out=cwbf[:, sl, :], in_=cwraw[:, sl, :], func=AF.Copy),
                            se, waits=[(QS["cv%d" % sl], tld), (pe, slot_mm[sl])])
                        slot_ld[sl] = tcast
                        lastw = None
                        for kc in range(8):
                            a = kh * 8 + kc
                            for m2 in range(2):
                                for th in range(2):
                                    fin = (kc == 7 and m2 == 1 and th == 1)
                                    lastw = P.do("tensor", lambda e, sl=sl, kc=kc, m2=m2, th=th, a=a, w=w, ps4=pss4, first=first: e.matmul(
                                        ps4[2 * m2 + th][:, :],
                                        cwbf[:, sl, 256 * kc + 128 * m2:256 * kc + 128 * m2 + 128],
                                        xln_fm[:, a, w + 512 * th: w + 512 * th + 512],
                                        start=first, stop=False),
                                        pe if fin else None,
                                        waits=[(se, tcast), (se, set_free[pset])])
                            first = False
                        slot_mm[sl] = lastw
                tmm = None
                for m2 in range(2):
                    for th in range(2):
                        tmm = P.do("tensor", lambda e, g=grp, m2=m2, th=th, ps4=pss4: e.matmul(
                            ps4[2 * m2 + th][:, :],
                            biasbf[0:1, OFF_CB + 256 * g + 128 * m2: OFF_CB + 256 * g + 128 * m2 + 128],
                            onesbf[0:1, :], start=False, stop=True),
                            pe if (m2 == 1 and th == 1) else None,
                            waits=[(ve, t_onesbf), (ve, t_biasbf)])
                tevl = None
                for m2 in range(2):
                    for th in range(2):
                        tevl = P.do("scalar", lambda e, g=grp, m2=m2, th=th, ps4=pss4: e.activation(
                            out=qkv_fm[:, 2 * g + m2, 512 * th:512 * th + 512],
                            in_=ps4[2 * m2 + th][:, :], func=AF.Copy), se, waits=[(pe, tmm)])
                set_free[pset] = tevl
            P.do("vector", lambda e: e.tensor_copy(
                out=haloL[:].rearrange("p (a o) -> p a o", o=1), in_=xln_fm[:, :, ACH:ACH + 1]), ve,
                waits=[(se, set_free[0]), (se, set_free[1])])
            P.barrier(ALLS)
            P.flush()
        A.free(cm_xln)

        wraw, cm_wraw = A.get([128, 2, 1024], F32, side="right")
        wbf, cm_wbf = A.get([128, 8, 1024], BF16, side="right")
        qh_fm, cm_qh = A.get([128, 8, ACH], BF16)
        kh_fm, cm_kh = A.get([128, 8, ACH], BF16)
        vh_tm, cm_vh = A.get([128, 8, 1024], BF16)
        with contextlib.ExitStack() as s3:
            qps = [s3.enter_context(pst("qps%d" % i, [128, 512], F32)) for i in range(4)]
            wcast_free = {0: 0, 1: 0}
            wbf_free = 0
            for wi, (wd, boff, scl) in enumerate([(wq_d, OFF_BQ, QSC), (wk_d, OFF_BK, 1.0), (wv_d, OFF_BV, 1.0)]):
                tcasts = []
                for a in range(8):
                    sl = a % 2
                    tld = P.do("sync", lambda e, wd=wd, a=a, sl=sl: e.dma_start(
                        out=wraw[:, sl, :], in_=wd[128 * a:128 * a + 128, :]), qsem("wr%d" % sl),
                        waits=[(se, tcasts[a - 2] if a >= 2 else wcast_free[sl])])
                    tc = P.do("scalar", lambda e, a=a, sl=sl: e.activation(
                        out=wbf[:, a, :], in_=wraw[:, sl, :], func=AF.Copy),
                        se, waits=[(QS["wr%d" % sl], tld), (pe, wbf_free)])
                    tcasts.append(tc)
                wcast_free = {0: tcasts[6], 1: tcasts[7]}
                set_free = [0, 0]
                if wi < 2:
                    dst_fm = qh_fm if wi == 0 else kh_fm
                    for m in range(8):
                        pset = m % 2
                        ps2 = qps[2 * pset:2 * pset + 2]
                        tmm = None
                        for th in range(2):
                            for a2 in range(8):
                                tmm = P.do("tensor", lambda e, m=m, th=th, a2=a2, ps2=ps2, wi=wi: e.matmul(
                                    ps2[th][:, :], wbf[:, a2, 128 * m:128 * m + 128],
                                    qkv_fm[:, 8 * wi + a2, 512 * th:512 * th + 512],
                                    start=(a2 == 0), stop=False), None,
                                    waits=[(se, tcasts[a2]), (se, set_free[pset])])
                            tmm = P.do("tensor", lambda e, m=m, th=th, boff=boff, ps2=ps2: e.matmul(
                                ps2[th][:, :], biasbf[0:1, boff + 128 * m:boff + 128 * m + 128],
                                onesbf[0:1, :], start=False, stop=True),
                                pe if th == 1 else None)
                        tev = None
                        for th in range(2):
                            tev = P.do("scalar", lambda e, m=m, th=th, scl=scl, dst=dst_fm, ps2=ps2: e.activation(
                                out=dst[:, m, 512 * th:512 * th + 512],
                                in_=ps2[th][:, :], func=AF.Copy, scale=scl), se, waits=[(pe, tmm)])
                        set_free[pset] = tev
                else:
                    for tt2 in range(8):
                        pset = tt2 % 2
                        ps2 = qps[2 * pset:2 * pset + 2]
                        tmm = None
                        for hh in range(2):
                            for a2 in range(8):
                                tmm = P.do("tensor", lambda e, tt2=tt2, hh=hh, a2=a2, ps2=ps2: e.matmul(
                                    ps2[hh][:, :], qkv_fm[:, 16 + a2, 128 * tt2:128 * tt2 + 128],
                                    wbf[:, a2, 512 * hh:512 * hh + 512],
                                    start=(a2 == 0), stop=False), None,
                                    waits=[(se, tcasts[a2]), (se, set_free[pset])])
                            tmm = P.do("tensor", lambda e, hh=hh, ps2=ps2: e.matmul(
                                ps2[hh][:, :], onesbf[0:1, 0:128],
                                biasbf[0:1, OFF_BV + 512 * hh:OFF_BV + 512 * hh + 512],
                                start=False, stop=True), pe if hh == 1 else None)
                        tev = None
                        for hh in range(2):
                            tev = P.do("scalar", lambda e, tt2=tt2, hh=hh, ps2=ps2: e.activation(
                                out=vh_tm[:, tt2, 512 * hh:512 * hh + 512],
                                in_=ps2[hh][:, :], func=AF.Copy), se, waits=[(pe, tmm)])
                        set_free[pset] = tev
                wbf_free = pe.n
            P.barrier(ALLS)
            P.flush()
        A.free(cm_wbf, cm_wraw)
        A.free(cm_qkv)

        o_fm, cm_ofm = A.get([128, 8, ACH], BF16, side="right")
        with contextlib.ExitStack() as s4:
            scs = s4.enter_context(sbt("scs", [128, 2, 128], F32))
            pf32 = s4.enter_context(sbt("pf32", [128, 2, 128], F32))
            pbf = s4.enter_context(sbt("pbf", [128, 2, 128], BF16))
            pTs = s4.enter_context(sbt("pTs", [128, 2, 128], BF16))
            otok = s4.enter_context(sbt("otok", [128, 2, 512], BF16))
            smst = s4.enter_context(sbt("smst", [128, 8], F32))
            sps = [s4.enter_context(pst("sps%d" % i, [128, 512], F32)) for i in range(2)]
            pps = s4.enter_context(pst("pps", [128, 512], BF16))
            ops = [s4.enter_context(pst("ops%d" % i, [128, 512], F32)) for i in range(2)]
            otps2 = [s4.enter_context(pst("otps%d" % i, [128, 512], BF16)) for i in range(2)]

            tok_sc = {}; tok_p = {}; tok_pT = {}; tok_o = {}; otps_free = {}
            tok_otok = 0
            for b in range(ACH // 128):
                for h in range(H):
                    i2 = (b * H + h) % 2
                    i4 = (b * H + h) % 4
                    hg = (h // 8) % 2
                    tsc = P.do("tensor", lambda e, b=b, h=h, i4=i4: e.matmul(
                        sps[i4 // 2][:, 128 * (i4 % 2):128 * (i4 % 2) + 128],
                        qh_fm[64 * (h % 2):64 * (h % 2) + 64, h // 2, 128 * b:128 * b + 128],
                        kh_fm[64 * (h % 2):64 * (h % 2) + 64, h // 2, 128 * b:128 * b + 128],
                        start=True, stop=True), pe,
                        waits=[(ve, tok_sc.get(i4, 0))])
                    tma = P.do("vector", lambda e, i2=i2, i4=i4: e.tensor_tensor(
                        out=scs[:, i2, :], in0=sps[i4 // 2][:, 128 * (i4 % 2):128 * (i4 % 2) + 128],
                        in1=mask32, op=ALU.add), ve,
                        waits=[(pe, tsc), (se, tok_p.get(i2, 0)), (QS["cq"], t_mask)])
                    tok_sc[i4] = tma
                    tmx = P.do("vector", lambda e, i2=i2: e.reduce_max(
                        out=smst[:, 4 * i2:4 * i2 + 1], in_=scs[:, i2, :], axis=AX, negate=True),
                        ve, waits=[(ve, tma)])
                    tex = P.do("scalar", lambda e, i2=i2: e.activation(
                        out=pf32[:, i2, :], in_=scs[:, i2, :], func=AF.Exp,
                        bias=smst[:, 4 * i2:4 * i2 + 1], scale=1.0,
                        accum_out=smst[:, 4 * i2 + 1:4 * i2 + 2]), se,
                        waits=[(ve, tmx), (se, tok_p.get(i2, 0))])
                    trc = P.do("vector", lambda e, i2=i2: e.reciprocal(
                        out=smst[:, 4 * i2 + 2:4 * i2 + 3], in_=smst[:, 4 * i2 + 1:4 * i2 + 2]),
                        ve, waits=[(se, tex)])
                    tpb = P.do("scalar", lambda e, i2=i2: e.activation(
                        out=pbf[:, i2, :], in_=pf32[:, i2, :], func=AF.Copy,
                        scale=smst[:, 4 * i2 + 2:4 * i2 + 3]), se,
                        waits=[(ve, trc), (pe, tok_pT.get(i2, 0))])
                    tok_p[i2] = tpb
                    tpt = P.do("tensor", lambda e, i2=i2: e.transpose(
                        pps[:, 128 * i2:128 * i2 + 128], pbf[:, i2, :], eyebf[:]), pe,
                        waits=[(se, tpb), (ve, tok_pT.get(("ev", i2), 0))])
                    tok_pT[i2] = tpt
                    tpe = P.do("vector", lambda e, i2=i2: e.tensor_copy(
                        out=pTs[:, i2, :], in_=pps[:, 128 * i2:128 * i2 + 128]), ve,
                        waits=[(pe, tpt)])
                    tok_pT[("ev", i2)] = tpe
                    to = P.do("tensor", lambda e, b=b, h=h, i2=i2, hg=hg: e.matmul(
                        ops[hg][:, 64 * (h % 8):64 * (h % 8) + 64], pTs[:, i2, :],
                        vh_tm[:, b, 64 * h:64 * h + 64], start=True, stop=True), pe,
                        waits=[(ve, tpe), (se, tok_o.get(hg, 0))])
                    if h % 8 == 7:
                        tevo = P.do("scalar", lambda e, hg=hg: e.activation(
                            out=otok[:, hg, :], in_=ops[hg][:, :], func=AF.Copy), se,
                            waits=[(pe, to), (ve, tok_otok)])
                        tok_o[hg] = tevo
                for g2 in range(2):
                    otp = otps2[g2]
                    ttp = None
                    for k4 in range(4):
                        ec = 4 * g2 + k4
                        ttp = P.do("tensor", lambda e, ec=ec, k4=k4, otp=otp: e.transpose(
                            otp[:, 128 * k4:128 * k4 + 128], otok[:, ec // 4, 128 * (ec % 4):128 * (ec % 4) + 128],
                            eyebf[:]), pe if k4 == 3 else None,
                            waits=[(se, tok_o[0]), (se, tok_o[1]), (ve, otps_free.get(g2, 0))])
                    dst = o_fm[:, 4 * g2:4 * g2 + 4, 128 * b:128 * b + 128]
                    tte = P.do("vector", lambda e, d=dst, otp=otp: e.tensor_copy(
                        out=d, in_=otp[:].rearrange("p (a n) -> p a n", n=128)), ve, waits=[(pe, ttp)])
                    otps_free[g2] = tte
                tok_otok = otps_free[1]
            P.barrier(ALLS)
            P.flush()
        A.free(cm_vh, cm_kh, cm_qh)

        mid_fm, cm_mid = A.get([128, 8, ACH], BF16)
        wraw, cm_wraw = A.get([128, 2, 1024], F32)
        wbf, cm_wbf = A.get([128, 8, 1024], BF16)
        with contextlib.ExitStack() as s5:
            qps = [s5.enter_context(pst("wps%d" % i, [128, 512], F32)) for i in range(4)]
            tcasts = []
            for a in range(8):
                sl = a % 2
                tld = P.do("sync", lambda e, a=a, sl=sl: e.dma_start(
                    out=wraw[:, sl, :], in_=wo_d[128 * a:128 * a + 128, :]), qsem("wr%d" % sl),
                    waits=[(se, tcasts[a - 2] if a >= 2 else 0), (pe, pe.n if a < 2 else 0)])
                tc = P.do("scalar", lambda e, a=a, sl=sl: e.activation(
                    out=wbf[:, a, :], in_=wraw[:, sl, :], func=AF.Copy), se,
                    waits=[(QS["wr%d" % sl], tld)])
                tcasts.append(tc)
            set_free = [0, 0]
            for m in range(8):
                pset = m % 2
                ps2 = qps[2 * pset:2 * pset + 2]
                tmm = None
                for th in range(2):
                    for a2 in range(8):
                        tmm = P.do("tensor", lambda e, m=m, th=th, a2=a2, ps2=ps2: e.matmul(
                            ps2[th][:, :], wbf[:, a2, 128 * m:128 * m + 128],
                            o_fm[:, a2, 512 * th:512 * th + 512], start=(a2 == 0), stop=False), None,
                            waits=[(se, tcasts[a2]), (se, set_free[pset])])
                    tmm = P.do("tensor", lambda e, m=m, th=th, ps2=ps2: e.matmul(
                        ps2[th][:, :], biasbf[0:1, OFF_BO + 128 * m:OFF_BO + 128 * m + 128],
                        onesbf[0:1, :], start=False, stop=True), pe if th == 1 else None)
                tev = None
                for th in range(2):
                    tev = P.do("scalar", lambda e, m=m, th=th, ps2=ps2: e.activation(
                        out=mid_fm[:, m, 512 * th:512 * th + 512], in_=ps2[th][:, :], func=AF.Copy),
                        se, waits=[(pe, tmm)])
                set_free[pset] = tev
            P.barrier(ALLS)
            P.flush()
        A.free(cm_wbf, cm_wraw)
        A.free(cm_ofm)

        with contextlib.ExitStack() as s6:
            pwb = s6.enter_context(sbt("pwb", [128, 2, 1024], F32))
            pwbf = s6.enter_context(sbt("pwbf", [128, 2, 1024], BF16))
            ptmp = s6.enter_context(sbt("ptmp", [128, 2, 512], F32))
            x2row = s6.enter_context(sbt("x2row", [128, 4, D], F32))
            xres = s6.enter_context(sbt("xres", [128, 2, D], F32))
            prps = [s6.enter_context(pst("prps%d" % i, [128, 512], F32)) for i in range(2)]
            trps = [s6.enter_context(pst("trps%d" % i, [128, 512], F32)) for i in range(2)]
            wtok = [0, 0]
            rtok = {}
            for th in range(2):
                for m in range(32):
                    sl = m % 2
                    tld = P.do("sync", lambda e, m=m, sl=sl: e.dma_start(
                        out=pwb[:, sl, :].rearrange("p (a n) -> p a n", n=128),
                        in_=pw_d[:, 128 * m:128 * m + 128].rearrange("(a p) n -> p a n", p=128)),
                        qsem("pw%d" % sl), waits=[(pe, wtok[sl])])
                    tc = P.do("scalar", lambda e, sl=sl: e.activation(
                        out=pwbf[:, sl, :], in_=pwb[:, sl, :], func=AF.Copy), se,
                        waits=[(QS["pw%d" % sl], tld)])
                    tmm = None
                    for a2 in range(8):
                        tmm = P.do("tensor", lambda e, sl=sl, a2=a2, th=th: e.matmul(
                            prps[sl][:, :], pwbf[:, sl, 128 * a2:128 * a2 + 128],
                            mid_fm[:, a2, 512 * th:512 * th + 512], start=(a2 == 0), stop=False),
                            None, waits=[(se, tc), (se, rtok.get(("ev", sl), 0))])
                    tmm = P.do("tensor", lambda e, m=m, sl=sl: e.matmul(
                        prps[sl][:, :], biasbf[0:1, OFF_PB + 128 * m:OFF_PB + 128 * m + 128],
                        onesbf[0:1, :], start=False, stop=True), pe)
                    wtok[sl] = tmm
                    tse = P.do("scalar", lambda e, sl=sl: e.activation(
                        out=ptmp[:, sl, :], in_=prps[sl][:, :], func=AF.Copy), se, waits=[(pe, tmm)])
                    rtok[("ev", sl)] = tse
                    ttp = None
                    for k4 in range(4):
                        ttp = P.do("tensor", lambda e, sl=sl, k4=k4: e.transpose(
                            trps[sl][:, 128 * k4:128 * k4 + 128],
                            ptmp[:, sl, 128 * k4:128 * k4 + 128], eye32),
                            pe if k4 == 3 else None,
                            waits=[(se, tse), (ve, rtok.get(("tr", sl), 0)), (QS["cq"], t_eye)])
                    dst = x2row[:, :, 128 * m:128 * m + 128]
                    ttv = P.do("vector", lambda e, sl=sl, d=dst: e.tensor_copy(
                        out=d, in_=trps[sl][:].rearrange("p (a n) -> p a n", n=128)), ve,
                        waits=[(pe, ttp)])
                    rtok[("tr", sl)] = ttv
                for tt3 in range(4):
                    xsl = tt3 % 2
                    trow = base_row + 512 * th + 128 * tt3
                    tlx = P.do("gpsimd", lambda e, trow=trow, xsl=xsl: e.dma_start(
                        out=xres[:, xsl, :], in_=xh[trow:trow + 128, :]), qsem("xr%d" % xsl),
                        waits=[(ve, rtok.get(("res", xsl), 0))])
                    tad = P.do("vector", lambda e, tt3=tt3, xsl=xsl: e.tensor_tensor(
                        out=x2row[:, tt3, :], in0=x2row[:, tt3, :], in1=xres[:, xsl, :], op=ALU.add),
                        ve, waits=[(QS["xr%d" % xsl], tlx), (ve, rtok[("tr", 0)]), (ve, rtok[("tr", 1)])])
                    rtok[("res", xsl)] = tad
                    x2_tok = P.do("gpsimd", lambda e, tt3=tt3, th=th, ch=ch: e.dma_start(
                        out=x2_s[ACH * ch + 512 * th + 128 * tt3: ACH * ch + 512 * th + 128 * tt3 + 128, :],
                        in_=x2row[:, tt3, :]), gq, waits=[(ve, tad)])
                if th == 0:
                    P.w("vector", gq, x2_tok)
            P.barrier(ALLS)
            P.flush()
        A.free(cm_mid)
    A.free(cm_biasbf)

    if STAGE in ("attn", "deq"):
        with contextlib.ExitStack() as sd:
            cp = sd.enter_context(sbt("cp", [128, D], F32))
            tprev = 0
            for i in range(TC // 128):
                tl = P.do("gpsimd", lambda e, i=i: e.dma_start(out=cp[:], in_=x2_s[128 * i:128 * i + 128, :]),
                          gq, waits=[(gq, tprev)])
                tprev = P.do("gpsimd", lambda e, i=i: e.dma_start(out=y_d[128 * i:128 * i + 128, :], in_=cp[:]),
                             gq, waits=[(gq, tl)])
            P.w("gpsimd", gq, gq.n)
            P.flush()
        es.close()
        return nc

    hsb, cm_hsb = A.get([128, NI, FCH], BF16)
    xgb, cm_xgb = A.get([32, FCH], BF16)
    hgb, cm_hgb = A.get([86, FCH], BF16)
    for fc in range(NF if FLIM >= 5 else 1):
        frow = FCH * fc
        xln2, cm_xln2 = A.get([128, 32, FCH], BF16, side="right")
        with contextlib.ExitStack() as s1:
            xbuf = s1.enter_context(sbt("fxbuf", [128, 2, D], F32))
            sqb = s1.enter_context(sbt("fsqb", [128, D], F32))
            stat = s1.enter_context(sbt("fstat", [128, 16], F32))
            xlntm = s1.enter_context(sbt("fxlntm", [128, 2, D], BF16))
            tps = [s1.enter_context(pst("ftps%d" % i, [128, 512], BF16)) for i in range(2)]
            xgps = s1.enter_context(pst("xgps", [128, 512], F32))
            P.barrier(ALLS)
            evtoks = {}
            ln_pass(lambda i: x2_s[frow + 128 * i: frow + 128 * (i + 1), :], FCH // 128,
                    lambda i, t7, half: ln_transposes(i, t7, half, xlntm, xln2, FCH, 0, tps, evtoks),
                    xbuf, sqb, stat, xlntm, "gpsimd", [qsem("xq0"), qsem("xq1")])
            txg = None
            for a in range(32):
                txg = P.do("tensor", lambda e, a=a: e.matmul(
                    xgps[0:32, :], c32bf[:, 32 * a:32 * a + 32], xln2[:, a, :],
                    start=(a == 0), stop=(a == 31)), pe if a == 31 else None,
                    waits=[(ve, t_c32bf), (ve, evtoks.get(0, 0)), (ve, evtoks.get(1, 0))])
            P.do("scalar", lambda e: e.activation(out=xgb[:, :], in_=xgps[0:32, :], func=AF.Copy),
                 se, waits=[(pe, txg)])
            P.barrier(ALLS)
            P.flush()

        if FLIM < 2:
            break
        with contextlib.ExitStack() as s2:
            gw = s2.enter_context(sbt("gw", [128, 2, 4096], BF16))
            uw = s2.enter_context(sbt("uw", [128, 2, 4096], BF16))
            szg_r = s2.enter_context(sbt("szg_r", [32, 2816], BF16))
            szu_r = s2.enter_context(sbt("szu_r", [32, 2816], BF16))
            silut = s2.enter_context(sbt("silut", [128, 2, 512], F32))
            gps = [s2.enter_context(pst("gps%d" % i, [128, 512], F32)) for i in range(2)]
            ups = [s2.enter_context(pst("ups%d" % i, [128, 512], F32)) for i in range(2)]
            gfree = [0, 0]; ufree = [0, 0]; hfree = {}
            sz_tok = [0] * 4
            for i in range(NI):
                sl = i % 2
                pp = i % 2
                quarter = i // 22
                if i % 22 == 0:
                    lo = quarter * 2816
                    ln2 = min(IN - lo, 2816)
                    P.do("scalar", lambda e, lo=lo, n=ln2: e.dma_start(
                        out=szg_r[:, 0:n], in_=szg_s[:, lo:lo + n]), qsem("szg"),
                        waits=[(pe, gfree[0]), (pe, gfree[1]), (pe, ufree[0]), (pe, ufree[1])])
                    sz_tok[quarter] = P.do("scalar", lambda e, lo=lo, n=ln2: e.dma_start(
                        out=szu_r[:, 0:n], in_=szu_s[:, lo:lo + n]), qsem("szu"))
                tgl = P.do("sync", lambda e, i=i, sl=sl: e.dma_start(
                    out=gw[:, sl, :], in_=wg_s[i, :, :]), qsem("gw%d" % sl), waits=[(pe, gfree[sl])])
                tul = P.do("sync", lambda e, i=i, sl=sl: e.dma_start(
                    out=uw[:, sl, :], in_=wu_s[i, :, :]), qsem("uw%d" % sl), waits=[(pe, ufree[sl])])
                for a in range(32):
                    P.do("tensor", lambda e, a=a, sl=sl, pp=pp: e.matmul(
                        gps[pp][:, :], gw[:, sl, 128 * a:128 * a + 128], xln2[:, a, :],
                        start=(a == 0), stop=False), None,
                        waits=[(QS["gw%d" % sl], tgl), (ve, hfree.get(pp, 0))])
                tg = P.do("tensor", lambda e, i=i, pp=pp, q=quarter: e.matmul(
                    gps[pp][:, :], szg_r[:, 128 * (i - q * 22):128 * (i - q * 22) + 128],
                    xgb[:, :], start=False, stop=True), pe,
                    waits=[(QS["szg"], QS["szg"].n), (QS["szu"], sz_tok[quarter])])
                gfree[sl] = tg
                for a in range(32):
                    P.do("tensor", lambda e, a=a, sl=sl, pp=pp: e.matmul(
                        ups[pp][:, :], uw[:, sl, 128 * a:128 * a + 128], xln2[:, a, :],
                        start=(a == 0), stop=False), None, waits=[(QS["uw%d" % sl], tul)])
                tu = P.do("tensor", lambda e, i=i, pp=pp, q=quarter: e.matmul(
                    ups[pp][:, :], szu_r[:, 128 * (i - q * 22):128 * (i - q * 22) + 128],
                    xgb[:, :], start=False, stop=True), pe)
                ufree[sl] = tu
                ts = P.do("scalar", lambda e, pp=pp: e.activation(
                    out=silut[:, pp, :], in_=gps[pp][:, :], func=AF.Silu), se,
                    waits=[(pe, tu), (ve, hfree.get(pp, 0))])
                th2 = P.do("vector", lambda e, i=i, pp=pp: e.tensor_tensor(
                    out=hsb[:, i, :], in0=silut[:, pp, :], in1=ups[pp][:, :], op=ALU.mult), ve,
                    waits=[(se, ts)])
                hfree[pp] = th2
            P.barrier(ALLS)
            P.flush()
        A.free(cm_xln2)

        if FLIM < 3:
            break
        with contextlib.ExitStack() as sh:
            c86t = sh.enter_context(sbt("c86t", [128, 86 * 86], F32))
            c86b = sh.enter_context(sbt("c86b", [128, 86 * 86], BF16))
            hgps = sh.enter_context(pst("hgps", [128, 512], F32))
            tl = P.do("scalar", lambda e: e.dma_start(out=c86t[:], in_=c86_d[:]), qsem("c86"))
            tc = P.do("vector", lambda e: e.tensor_copy(out=c86b[:], in_=c86t[:]), ve, waits=[(QS["c86"], tl)])
            th3 = None
            for a in range(NI):
                th3 = P.do("tensor", lambda e, a=a: e.matmul(
                    hgps[0:86, :], c86b[:, 86 * a:86 * a + 86], hsb[:, a, :],
                    start=(a == 0), stop=(a == NI - 1)), pe if a == NI - 1 else None,
                    waits=[(ve, tc)])
            P.do("scalar", lambda e: e.activation(out=hgb[:, :], in_=hgps[0:86, :], func=AF.Copy),
                 se, waits=[(pe, th3)])
            P.barrier(ALLS)
            P.flush()

        if FLIM < 4:
            break
        with contextlib.ExitStack() as s3:
            dw = s3.enter_context(sbt("dw", [128, 2, 5504], BF16))
            szd_r = s3.enter_context(sbt("szd_r", [86, 1024], BF16))
            dtmp = s3.enter_context(sbt("dtmp", [128, 2, 512], BF16))
            yrow = s3.enter_context(sbt("yrow", [128, 4, D], F32))
            xres2 = s3.enter_context(sbt("xres2", [128, D], F32))
            dps = [s3.enter_context(pst("dps%d" % i, [128, 512], F32)) for i in range(2)]
            dtps = [s3.enter_context(pst("dtps%d" % i, [128, 512], BF16)) for i in range(2)]
            for _k in range(2):
                qsem("yq%d" % _k); qsem("xp%d" % _k)
            slot_free = [0, 0]
            sz_tok4 = [0] * 4
            evtok = {}
            trtok = {0: 0, 1: 0}
            for m in range(32):
                pp = m % 2
                if m % 8 == 0:
                    sz_tok4[m // 8] = P.do("scalar", lambda e, m=m: e.dma_start(
                        out=szd_r[:, :], in_=szd_s[:, 128 * m:128 * m + 1024]), qsem("szd"),
                        waits=[(pe, pe.n)])
                tmm = None
                for hf in range(2):
                    sl = hf
                    a0, a1 = (0, 43) if hf == 0 else (43, 86)
                    tld = P.do("sync", lambda e, m=m, sl=sl, a0=a0, a1=a1: e.dma_start(
                        out=dw[:, sl, 0:(a1 - a0) * 128], in_=wd_s[m, :, 128 * a0:128 * a1]),
                        qsem("dw%d" % sl), waits=[(pe, slot_free[sl])])
                    for a in range(a0, a1):
                        fin = (a == a1 - 1)
                        tmm = P.do("tensor", lambda e, sl=sl, a=a, a0=a0, pp=pp, m=m: e.matmul(
                            dps[pp][:, :], dw[:, sl, 128 * (a - a0):128 * (a - a0) + 128],
                            hsb[:, a, :], start=(a == 0), stop=False),
                            pe if fin else None,
                            waits=[(QS["dw%d" % sl], tld), (se, evtok.get(pp, 0))])
                    slot_free[sl] = tmm
                tmm = P.do("tensor", lambda e, m=m, pp=pp: e.matmul(
                    dps[pp][:, :], szd_r[:, 128 * (m % 8):128 * (m % 8) + 128], hgb[:, :],
                    start=False, stop=True), pe, waits=[(QS["szd"], sz_tok4[m // 8])])
                tse = P.do("scalar", lambda e, pp=pp: e.activation(
                    out=dtmp[:, pp, :], in_=dps[pp][:, :], func=AF.Copy), se, waits=[(pe, tmm)])
                evtok[pp] = tse
                ttp = None
                for k4 in range(4):
                    ttp = P.do("tensor", lambda e, pp=pp, k4=k4: e.transpose(
                        dtps[pp][:, 128 * k4:128 * k4 + 128], dtmp[:, pp, 128 * k4:128 * k4 + 128],
                        eyebf[:]), pe if k4 == 3 else None,
                        waits=[(se, tse), (ve, trtok[pp]), (ve, t_eyebf)])
                dsty = yrow[:, :, 128 * m:128 * m + 128]
                ttv = P.do("vector", lambda e, pp=pp, d=dsty: e.tensor_copy(
                    out=d, in_=dtps[pp][:].rearrange("p (a n) -> p a n", n=128)), ve,
                    waits=[(pe, ttp)])
                trtok[pp] = ttv
            prev_add = 0
            for tt3 in range(4):
                trow = frow + 128 * tt3
                tlx = P.do("gpsimd", lambda e, trow=trow: e.dma_start(
                    out=xres2[:, :], in_=x2_s[trow:trow + 128, :]), qsem("xp0"),
                    waits=[(ve, trtok[0]), (ve, trtok[1]), (ve, prev_add)])
                tad = P.do("vector", lambda e, tt3=tt3: e.tensor_tensor(
                    out=yrow[:, tt3, :], in0=yrow[:, tt3, :], in1=xres2[:, :], op=ALU.add),
                    ve, waits=[(QS["xp0"], tlx)])
                prev_add = tad
                P.do("gpsimd", lambda e, trow=trow, tt3=tt3: e.dma_start(
                    out=y_d[trow:trow + 128, :], in_=yrow[:, tt3, :]), qsem("yq%d" % (tt3 % 2)),
                    waits=[(ve, tad)])
            P.barrier(ALLS)
            P.flush()
    A.free(cm_hgb, cm_xgb, cm_hsb)

    if FLIM < 4:
        with contextlib.ExitStack() as sd:
            cp = sd.enter_context(sbt("cpd", [128, D], F32))
            tprev = 0
            for i in range(TC // 128):
                tl = P.do("gpsimd", lambda e, i=i: e.dma_start(out=cp[:], in_=x2_s[128 * i:128 * i + 128, :]),
                          gq, waits=[(gq, tprev)])
                tprev = P.do("gpsimd", lambda e, i=i: e.dma_start(out=y_d[128 * i:128 * i + 128, :], in_=cp[:]),
                             gq, waits=[(gq, tl)])
            P.w("gpsimd", gq, gq.n)
            P.flush()

    P.barrier(ALLS)
    P.flush()
    es.close()
    return nc


_NC_CACHE = {}


def _get_nc():
    if STAGE not in _NC_CACHE:
        _NC_CACHE[STAGE] = build_nc()
    return _NC_CACHE[STAGE]


def make_inmaps(inputs):
    x = np.ascontiguousarray(inputs["L0_attn_inputs"])
    tri = np.tril(np.ones((128, 128), bool))
    mask = np.where(tri, 0.0, -1e9).astype(np.float32)
    eye = np.eye(128, dtype=np.float32)
    ones = np.ones((1, 512), np.float32)
    c32 = np.zeros((128, 32 * 32), np.float32)
    for a in range(32):
        c32[:, 32 * a + a] = 1.0
    c86 = np.zeros((128, 86 * 86), np.float32)
    for a in range(86):
        c86[:, 86 * a + a] = 1.0
    band = np.zeros((8, 128), np.float32)
    for u in range(8):
        band[u, 16 * u:16 * u + 16] = 1.0
    common = {
        "conv_w": np.ascontiguousarray(inputs["L0_attn_conv_w"]),
        "conv_b": np.ascontiguousarray(inputs["L0_attn_conv_b"]).reshape(1, -1),
        "wq": np.ascontiguousarray(inputs["L0_attn_wq"]).reshape(WQ, H * DH),
        "bq": np.ascontiguousarray(inputs["L0_attn_bq"]).reshape(1, -1),
        "wk": np.ascontiguousarray(inputs["L0_attn_wk"]).reshape(WQ, H * DH),
        "bk": np.ascontiguousarray(inputs["L0_attn_bk"]).reshape(1, -1),
        "wv": np.ascontiguousarray(inputs["L0_attn_wv"]).reshape(WQ, H * DH),
        "bv": np.ascontiguousarray(inputs["L0_attn_bv"]).reshape(1, -1),
        "wo": np.ascontiguousarray(inputs["L0_attn_wo"]).reshape(H * DH, WQ),
        "bo": np.ascontiguousarray(inputs["L0_attn_bo"]).reshape(1, -1),
        "pw": np.ascontiguousarray(inputs["L0_attn_proj_w"]),
        "pb": np.ascontiguousarray(inputs["L0_attn_proj_b"]).reshape(1, -1),
        "g_qw": np.ascontiguousarray(inputs["L0_ffn_gate_qweight"]),
        "g_sc": np.ascontiguousarray(inputs["L0_ffn_gate_scales"]),
        "g_qz": np.ascontiguousarray(inputs["L0_ffn_gate_qzeros"]),
        "u_qw": np.ascontiguousarray(inputs["L0_ffn_up_qweight"]),
        "u_sc": np.ascontiguousarray(inputs["L0_ffn_up_scales"]),
        "u_qz": np.ascontiguousarray(inputs["L0_ffn_up_qzeros"]),
        "d_qw": np.ascontiguousarray(inputs["L0_ffn_down_qweight"]),
        "d_sc": np.ascontiguousarray(inputs["L0_ffn_down_scales"]),
        "d_qz": np.ascontiguousarray(inputs["L0_ffn_down_qzeros"]),
        "eye": eye, "mask": mask, "ones": ones, "c32": c32, "c86": c86, "band": band,
    }
    in_maps = []
    for c in range(8):
        b, s0 = c // 2, (c % 2) * TC
        shd = np.zeros((TC + 2, D), np.float32)
        shd[1:TC + 1] = x[b, s0:s0 + TC]
        if s0 > 0:
            shd[0] = x[b, s0 - 1]
        if s0 + TC < 4096:
            shd[TC + 1] = x[b, s0 + TC]
        mm = dict(common)
        mm["xh"] = shd
        in_maps.append(mm)
    return in_maps


def kernel(**inputs):
    nc = _get_nc()
    in_maps = make_inmaps(inputs)
    res = run_bass_kernel_spmd(nc, in_maps, core_ids=list(range(8)))
    out = np.empty((4, 4096, D), np.float32)
    for c in range(8):
        out[c // 2, (c % 2) * TC:(c % 2) * TC + TC] = res.results[c]["y"]
    return out
